# version 02 (final)
# baseline (speedup 1.0000x reference)
import numpy as np
from contextlib import ExitStack
import concourse.bass as bass
import concourse.mybir as mybir
from concourse.bass_utils import run_bass_kernel_spmd

F32 = mybir.dt.float32
I32 = mybir.dt.int32
AF = mybir.ActivationFunctionType
OP = mybir.AluOpType
AX = mybir.AxisListType

NCORE = 8
T = 2048
D = 1024
NT = 16
EPS_LN = 1e-5


from concourse.bass_types import AP as _APT


def _ap(x):
    if x is None or isinstance(x, (_APT, int, float)):
        return x
    try:
        return x[:]
    except Exception:
        return x


class PB:

    def __init__(self):
        self.nc = bass.Bass("TRN2", target_bir_lowering=False)
        self.ctx = ExitStack()
        self.sched = []
        self.cnt = 0
        self.tick = self.ctx.enter_context(self.nc.semaphore("tick"))
        self.P0 = self.ctx.enter_context(self.nc.psum_tensor("P0", [128, 512], F32))
        self.P1 = self.ctx.enter_context(self.nc.psum_tensor("P1", [128, 512], F32))
        self.PBp = self.ctx.enter_context(self.nc.psum_tensor("PBp", [128, 512], F32))
        self.PT = self.ctx.enter_context(self.nc.psum_tensor("PT", [128, 128], F32))
        self.PZa = self.ctx.enter_context(self.nc.psum_tensor("PZa", [128, 8], F32))
        self.PZb = self.ctx.enter_context(self.nc.psum_tensor("PZb", [128, 8], F32))
        self._eye = None
        self._ones = None

    def sb(self, name, shape, dt=F32):
        return self.ctx.enter_context(self.nc.sbuf_tensor(name, shape, dt))

    def inp(self, name, shape):
        return self.nc.dram_tensor(name, shape, F32, kind="ExternalInput").ap()

    def outp(self, name, shape):
        return self.nc.dram_tensor(name, shape, F32, kind="ExternalOutput").ap()

    def dram(self, name, shape):
        return self.nc.dram_tensor(name, shape, F32, kind="Internal").ap()

    def consts(self):
        self.eye = self.sb("c_eye", [128, 128])
        self.ones = self.sb("c_ones", [1, 128])
        self.onescol = self.sb("c_onescol", [128, 1])
        self.e_eye = self.inp("c_eye_i", [128, 128])
        self.e_ones = self.inp("c_ones_i", [1, 128])
        self.e_onescol = self.inp("c_onescol_i", [128, 1])
        self.dma(self.eye[:, :], self.e_eye)
        self.dma(self.ones[:, :], self.e_ones)
        self.dma(self.onescol[:, :], self.e_onescol)

    def _q(self, eng, fn, inc=1):
        self.sched.append((eng, fn, self.cnt, inc))
        self.cnt += inc

    def dma(self, dst, src):
        dst = _ap(dst); src = _ap(src)
        self._q("sync", lambda e, a=dst, b=src: e.dma_start(out=a, in_=b), 16)

    def mm(self, out, lhsT, rhs, start=True, stop=True):
        out = _ap(out); lhsT = _ap(lhsT); rhs = _ap(rhs)
        self._q("tensor", lambda e, o=out, l=lhsT, r=rhs, s=start, p=stop:
                e.matmul(o, l, r, start=s, stop=p))

    def act(self, out, in_, f, bias=None, scale=None):
        out = _ap(out); in_ = _ap(in_); bias = _ap(bias); scale = _ap(scale)
        def fn(e, o=out, i=in_, f=f, b=bias, s=scale):
            kw = {}
            if b is not None:
                kw["bias"] = b
            if s is not None:
                kw["scale"] = s
            return e.activation(out=o, in_=i, func=f, **kw)
        self._q("scalar", fn)

    def tt(self, out, a, b, op):
        out = _ap(out); a = _ap(a); b = _ap(b)
        self._q("vector", lambda e, o=out, x=a, y=b, p=op:
                e.tensor_tensor(out=o, in0=x, in1=y, op=p))

    def ts(self, out, in0, s1, op0, s2=None, op1=None):
        out = _ap(out); in0 = _ap(in0); s1 = _ap(s1); s2 = _ap(s2)
        def fn(e, o=out, i=in0, a=s1, b=s2, p0=op0, p1=op1):
            kw = {}
            if p1 is not None:
                kw["op1"] = p1
            return e.tensor_scalar(out=o, in0=i, scalar1=a, scalar2=b, op0=p0, **kw)
        self._q("vector", fn)

    def red(self, out, in_, op):
        out = _ap(out); in_ = _ap(in_)
        self._q("vector", lambda e, o=out, i=in_, p=op:
                e.tensor_reduce(o, i, AX.X, p))

    def cp(self, out, in_):
        out = _ap(out); in_ = _ap(in_)
        self._q("vector", lambda e, o=out, i=in_: e.tensor_copy(o, i))

    def scp(self, out, in_):
        out = _ap(out); in_ = _ap(in_)
        self._q("scalar", lambda e, o=out, i=in_: e.copy(out=o, in_=i))

    def mset(self, ap, c):
        ap = _ap(ap)
        self._q("vector", lambda e, a=ap, v=c: e.memset(a, v))

    def recip(self, out, in_):
        out = _ap(out); in_ = _ap(in_)
        self._q("vector", lambda e, o=out, i=in_: e.reciprocal(o, i))

    def tr(self, out_sb, in_sb, n_out=128, n_in=128):
        out_sb = _ap(out_sb); in_sb = _ap(in_sb)
        self._q("tensor", lambda e, i=in_sb, n=n_out, m=n_in:
                e.transpose(self.PT[:n, :m], i, self.eye[:m, :m]))
        self.cp(out_sb, self.PT[:n_out, :n_in])

    def bc(self, dst, row, w, npart=128):
        dst = _ap(dst); row = _ap(row)
        self._q("tensor", lambda e, r=row, n=npart, ww=w:
                e.matmul(self.PBp[:n, :ww], self.ones[:1, :n], r, start=True, stop=True))
        self.cp(dst, self.PBp[:npart, :w])

    def finish(self):
        total = self.cnt
        engines = ["sync", "tensor", "vector", "scalar"]
        with self.nc.Block() as blk:
            for eng in engines:
                items = [it for it in self.sched if it[0] == eng]
                dec = getattr(blk, eng)

                def make(items_, eng_):
                    def _(e):
                        for (_, fn, w, inc) in items_:
                            if w > 0:
                                e.wait_ge(self.tick, w)
                            fn(e).then_inc(self.tick, inc)
                        e.wait_ge(self.tick, total)
                    return _
                dec(make(items, eng))
        return self.nc


def emit_ln(p, X, XNT, scr1, scr2, make_xnt=True, store_xn=None):
    mx = p.sb_mx
    for t in range(NT):
        xt = X[:, t, :]
        p.red(mx[:, 0:1], xt, OP.add)
        p.ts(mx[:, 0:1], mx[:, 0:1], 1.0 / D, OP.mult)
        p.ts(scr1, xt, mx[:, 0:1], OP.subtract)
        p.act(scr2, scr1[:, 0:512], AF.Square)
        p.red(mx[:, 1:2], scr2, OP.add)
        p.act(scr2, scr1[:, 512:1024], AF.Square)
        p.red(mx[:, 3:4], scr2, OP.add)
        p.tt(mx[:, 1:2], mx[:, 1:2], mx[:, 3:4], OP.add)
        p.ts(mx[:, 1:2], mx[:, 1:2], 1.0 / D, OP.mult, s2=EPS_LN, op1=OP.add)
        p.act(mx[:, 2:3], mx[:, 1:2], AF.Sqrt)
        p.recip(mx[:, 2:3], mx[:, 2:3])
        p.ts(scr1, scr1, mx[:, 2:3], OP.mult)
        if store_xn is not None:
            p.cp(store_xn[:, t, :], scr1)
        if make_xnt:
            for kc in range(8):
                p.tr(XNT[:, kc, t * 128:(t + 1) * 128], scr1[:, kc * 128:(kc + 1) * 128])


def build_attn_a():
    p = PB()
    xb = p.inp("xb", [T, D])
    wq = p.inp("wq", [D, D]); wk = p.inp("wk", [D, D]); wv = p.inp("wv", [D, D])
    bq = p.inp("bq", [1, D]); bk = p.inp("bk", [1, D]); bv = p.inp("bv", [1, D])
    beta = p.inp("beta", [1, 8])
    amask = p.inp("amask", [128, 896])
    yout = p.outp("y", [T, D])

    X = p.sb("X", [128, NT, 1024])
    XNT = p.sb("XNT", [128, 8, 2048])
    QT = p.sb("QT", [128, NT, 128]); SQT = p.sb("SQT", [128, NT, 128])
    KT = p.sb("KT", [128, NT, 128]); SKT = p.sb("SKT", [128, NT, 128])
    SKM = p.sb("SKM", [128, NT, 128]); VM = p.sb("VM", [128, NT, 128])
    AME = p.sb("AME", [128, 4, 128])
    M = p.sb("M", [128, 128]); Zc = p.sb("Zc", [128, 1])
    DEN = p.sb("DEN", [128, 1])
    SC = p.sb("SC", [128, 512])
    MSK = p.sb("MSK", [128, 896])
    PTsb = p.sb("PTsb", [128, 4, 128])
    Wsl = p.sb("Wsl", [128, 4, 128])
    Brow = p.sb("Brow", [1, 3 * 1024])
    BB = p.sb("BB", [128, 128])
    G8 = p.sb("G8", [128, 8]); G8i = p.sb("G8i", [128, 8])
    Bt = p.sb("Bt", [1, 8])
    s1 = p.sb("s1", [128, 1024]); s2 = p.sb("s2", [128, 512])
    T1 = s1[:, 0:128]; T2 = s1[:, 128:256]
    RT = s1[:, 256:384]; UU = s1[:, 384:512]
    p.sb_mx = p.sb("mx", [128, 4])
    p.consts()

    p.dma(X[:, :, :], xb.rearrange("(c p) n -> p c n", p=128))
    p.dma(Brow[:, 0:1024], bq); p.dma(Brow[:, 1024:2048], bk); p.dma(Brow[:, 2048:3072], bv)
    p.dma(Bt[:, :], beta)
    p.dma(MSK[:, :], amask)

    emit_ln(p, X, XNT, s1, s2)
    p.act(Bt[:, :], Bt[:, :], AF.Sigmoid)
    p.bc(G8[:, :], Bt[0:1, :], w=8)
    p.ts(G8i[:, :], G8[:, :], -1.0, OP.mult, s2=1.0, op1=OP.add)

    inv_sqrt_d = 1.0 / np.sqrt(128.0)
    for h in range(8):
        hsl = slice(h * 128, h * 128 + 128)
        for wi, (w, boff, kind) in enumerate([(wq, 0, "q"), (wk, 1024, "k"), (wv, 2048, "v")]):
            p.bc(BB[:, :], Brow[0:1, boff + h * 128: boff + h * 128 + 128], w=128)
            for t in range(NT):
                for hk in range(2):
                    p.dma(Wsl[:, :, :], w[hk * 512:(hk + 1) * 512, hsl]
                          .rearrange("(c p) n -> p c n", p=128))
                    for kc in range(4):
                        p.mm(p.P0[:, 0:128], XNT[:, hk * 4 + kc, t * 128:(t + 1) * 128],
                             Wsl[:, kc, :],
                             start=(hk == 0 and kc == 0), stop=(hk == 1 and kc == 3))
                if kind == "v":
                    p.tt(VM[:, t, :], p.P0[:, 0:128], BB[:, :], OP.add)
                else:
                    p.tt(T1[:, :], p.P0[:, 0:128], BB[:, :], OP.add)
                    dstT = QT if kind == "q" else KT
                    p.tr(dstT[:, t, :], T1[:, :])
                    p.ts(T2[:, :], T1[:, :], 0.0, OP.min)
                    p.act(T2[:, :], T2[:, :], AF.Exp)
                    p.act(RT[:, :], T1[:, :], AF.Relu)
                    p.tt(T2[:, :], T2[:, :], RT[:, :], OP.add)
                    if kind == "q":
                        p.tr(SQT[:, t, :], T2[:, :])
                    else:
                        p.cp(SKM[:, t, :], T2[:, :])
                        p.tr(SKT[:, t, :], T2[:, :])
        p.mset(M[:, :], 0.0)
        p.mset(Zc[:, :], 0.0)
        for s in range(4):
            base = 4 * s
            for c in range(4):
                i = base + c
                p.mm(p.P0[:, 0:128], SQT[:, i, :], M[:, :])
                p.mm(p.PZa[:, 0:1], SQT[:, i, :], Zc[:, :])
                p.ts(DEN[:, :], p.PZa[:, 0:1], 1e-6, OP.add)
                p.recip(DEN[:, :], DEN[:, :])
                p.ts(AME[:, c, :], p.P0[:, 0:128], DEN[:, 0:1], OP.mult)
                p.mm(p.P0[:, 0:128], SKT[:, i, :], M[:, :])
                p.mm(p.PZa[:, 0:1], SKT[:, i, :], Zc[:, :])
                p.ts(DEN[:, :], p.PZa[:, 0:1], 1e-6, OP.add)
                p.recip(DEN[:, :], DEN[:, :])
                p.ts(RT[:, :], p.P0[:, 0:128], DEN[:, 0:1], OP.mult)
                p.tt(UU[:, :], VM[:, i, :], RT[:, :], OP.subtract)
                p.mm(p.P1[:, 0:128], SKM[:, i, :], UU[:, :], start=(c == 0), stop=(c == 3))
                p.mm(p.PZb[:, 0:1], SKM[:, i, :], p.onescol[:, :], start=(c == 0), stop=(c == 3))
            for c in range(4):
                i = base + c
                p.mm(p.P0[:, 0:512], QT[:, i, :],
                     KT[:, base:base + 4, :].rearrange("p c n -> p (c n)"))
                p.ts(SC[:, :], p.P0[:, 0:512], inv_sqrt_d, OP.mult)
                p.tt(SC[:, :], SC[:, :], MSK[:, (3 - c) * 128:(3 - c) * 128 + 512], OP.add)
                p.red(p.sb_mx[:, 0:1], SC[:, :], OP.max)
                p.ts(SC[:, :], SC[:, :], p.sb_mx[:, 0:1], OP.subtract)
                p.act(SC[:, :], SC[:, :], AF.Exp)
                p.red(p.sb_mx[:, 1:2], SC[:, :], OP.add)
                p.recip(p.sb_mx[:, 1:2], p.sb_mx[:, 1:2])
                p.ts(SC[:, :], SC[:, :], p.sb_mx[:, 1:2], OP.mult)
                for j in range(4):
                    p.tr(PTsb[:, j, :], SC[:, j * 128:(j + 1) * 128])
                for j in range(4):
                    p.mm(p.P0[:, 0:128], PTsb[:, j, :], VM[:, base + j, :],
                         start=(j == 0), stop=(j == 3))
                p.ts(T1[:, :], AME[:, c, :], G8[:, h:h + 1], OP.mult)
                p.ts(T2[:, :], p.P0[:, 0:128], G8i[:, h:h + 1], OP.mult)
                p.tt(T1[:, :], T1[:, :], T2[:, :], OP.add)
                p.tt(X[:, i, hsl], X[:, i, hsl], T1[:, :], OP.add)
            p.tt(M[:, :], M[:, :], p.P1[:, 0:128], OP.add)
            p.tt(Zc[:, :], Zc[:, :], p.PZb[:, 0:1], OP.add)

    p.dma(yout.rearrange("(c p) n -> p c n", p=128), X[:, :, :])
    return p.finish()


def build_f1():
    p = PB()
    xb = p.inp("xb", [T, D])
    wr = p.inp("wr", [D, 64])
    br = p.inp("br", [1, 64])
    xn_o = p.outp("xn", [T, D])
    sc_o = p.outp("sc", [T, 64])

    X = p.sb("X", [128, NT, 1024])
    XNT = p.sb("XNT", [128, 8, 2048])
    XN = p.sb("XN", [128, NT, 1024])
    Wr = p.sb("Wr", [128, 8, 64])
    Brr = p.sb("Brr", [1, 64])
    BB = p.sb("BB", [128, 64])
    SCt = p.sb("SCt", [128, NT, 64])
    s1 = p.sb("s1", [128, 1024]); s2 = p.sb("s2", [128, 512])
    p.sb_mx = p.sb("mx", [128, 4])
    p.consts()

    p.dma(X[:, :, :], xb.rearrange("(c p) n -> p c n", p=128))
    p.dma(Wr[:, :, :], wr.rearrange("(c p) n -> p c n", p=128))
    p.dma(Brr[:, :], br)
    emit_ln(p, X, XNT, s1, s2, store_xn=XN)
    p.bc(BB[:, :], Brr[0:1, :], w=64)
    for t in range(NT):
        for kc in range(8):
            p.mm(p.P0[:, 0:64], XNT[:, kc, t * 128:(t + 1) * 128], Wr[:, kc, :],
                 start=(kc == 0), stop=(kc == 7))
        p.tt(SCt[:, t, :], p.P0[:, 0:64], BB[:, :], OP.add)
        p.red(p.sb_mx[:, 0:1], SCt[:, t, :], OP.max)
        p.ts(SCt[:, t, :], SCt[:, t, :], p.sb_mx[:, 0:1], OP.subtract)
        p.act(SCt[:, t, :], SCt[:, t, :], AF.Exp)
        p.red(p.sb_mx[:, 1:2], SCt[:, t, :], OP.add)
        p.recip(p.sb_mx[:, 1:2], p.sb_mx[:, 1:2])
        p.ts(SCt[:, t, :], SCt[:, t, :], p.sb_mx[:, 1:2], OP.mult)
    p.dma(xn_o.rearrange("(c p) n -> p c n", p=128), XN[:, :, :])
    p.dma(sc_o.rearrange("(c p) n -> p c n", p=128), SCt[:, :, :])
    return p.finish()


def build_f2():
    p = PB()
    NTOK = NCORE * T
    sc8 = p.inp("sc8", [8, NTOK])
    gs_o = p.outp("gs8", [8, 256])
    bv_o = p.outp("bv8", [8, 256])
    S = p.sb("S", [8, NTOK])
    SC1 = p.sb("SC1", [8, NTOK // 2])
    SC2 = p.sb("SC2", [8, NTOK // 2])
    GS = p.sb("GS", [8, 256])
    BV = p.sb("BV", [8, 256])
    p.consts()
    p.dma(S[:, :], sc8)

    def cmpex(A, B, cnt):
        o1 = SC1[:, 0:cnt]
        o2 = SC2[:, 0:cnt]
        shp = None
        a_shape = A
        p.tt(o1.rearrange("q (x) -> q x"), A, B, OP.max) if False else None
        p.tt(o1, A, B, OP.max)
        p.tt(o2, A, B, OP.min)
        p.cp(A, o1)
        p.cp(B, o2)

    n = NTOK
    pp = 1
    while pp < n:
        pp *= 2
        P2 = pp // 2
        v = S[:, :].rearrange("q (nb two k) -> q nb two k", two=2, k=P2)
        nb = n // (2 * P2)
        A = v[:, :, 0, :]
        B = v[:, :, 1, :]
        cnt = nb * P2
        o1 = SC1[:, 0:cnt].rearrange("q (nb k) -> q nb k", k=P2)
        o2 = SC2[:, 0:cnt].rearrange("q (nb k) -> q nb k", k=P2)
        p.tt(o1, A, B, OP.max); p.tt(o2, A, B, OP.min)
        p.cp(A, o1); p.cp(B, o2)
        k = P2 // 2
        while k >= 1:
            m = 2 * P2
            v3 = S[:, :].rearrange("q (nb m) -> q nb m", m=m)[:, :, k:m - k]
            v4 = v3.rearrange("q nb (r twok) -> q nb r twok", twok=2 * k)
            nb2 = n // m
            r = (m - 2 * k) // (2 * k)
            A = v4[:, :, :, 0:k]
            B = v4[:, :, :, k:2 * k]
            cnt = nb2 * r * k
            o1 = SC1[:, 0:cnt].rearrange("q (nb r k) -> q nb r k", r=r, k=k)
            o2 = SC2[:, 0:cnt].rearrange("q (nb r k) -> q nb r k", r=r, k=k)
            p.tt(o1, A, B, OP.max); p.tt(o2, A, B, OP.min)
            p.cp(A, o1); p.cp(B, o2)
            k //= 2

    p.red(GS[:, :].rearrange("q (g o) -> q g o", o=1),
          S[:, :].rearrange("q (g s) -> q g s", s=64), OP.add)
    p.cp(BV[:, :].rearrange("q (g o) -> q g o", o=1),
         S[:, :].rearrange("q (g s) -> q g s", s=64)[:, :, 63:64])
    p.dma(gs_o, GS[:, :])
    p.dma(bv_o, BV[:, :])
    return p.finish()


def build_f3():
    p = PB()
    NTOK = NCORE * T
    xn = p.inp("xn", [T, D])
    sc = p.inp("sc", [T, 64])
    gsf = p.inp("gsf", [64, 256])
    bvf = p.inp("bvf", [64, 256])
    resid = p.inp("resid", [T, D])
    wup = p.inp("wup", [64, 1024, 256])
    wgate = p.inp("wgate", [64, 1024, 256])
    wdown = p.inp("wdown", [64, 256, 1024])
    bup = p.inp("bup", [64, 256])
    bgate = p.inp("bgate", [64, 256])
    bdown = p.inp("bdown", [64, 1024])
    yo = p.outp("y", [T, D])

    HALF = T // 2
    NTH = 8
    X = p.sb("X", [128, NTH, 1024])
    XNT = p.sb("XNT", [128, 8, HALF])
    OT = p.sb("OT", [128, 8, HALF])
    H = p.sb("H", [128, 2, HALF])
    SCt = p.sb("SCt", [128, NT, 64])
    FST = p.sb("FST", [64, NT, 128])
    GS64 = p.sb("GS64", [64, 256]); BV64 = p.sb("BV64", [64, 256])
    LOHI = p.sb("LOHI", [1, 8])
    CN = p.sb("CN", [64, 4])
    KP = p.sb("KP", [64, 256]); KS1 = p.sb("KS1", [64, 256]); KS2 = p.sb("KS2", [64, 256])
    TH = p.sb("TH", [64, 1]); THR = p.sb("THR", [1, 64]); THB = p.sb("THB", [128, 64])
    F0 = p.sb("F0", [128, 64]); RS = p.sb("RS", [128, 1])
    Wu = p.sb("Wu", [128, 8, 256]); Wg = p.sb("Wg", [128, 8, 256])
    Wd = p.sb("Wd", [128, 2, 1024])
    buc = p.sb("buc", [128, 2]); bgc = p.sb("bgc", [128, 2])
    BR2 = p.sb("BR2", [1, 512])
    bdn = p.sb("bdn", [64, 1024])
    frow = p.sb("frow", [1, 2048])
    FB = p.sb("FB", [128, 512])
    UPt = p.sb("UPt", [128, 512]); GTt = p.sb("GTt", [128, 512])
    s1 = p.sb("s1", [128, 1024]); s2 = p.sb("s2", [128, 1024])
    p.sb_mx = p.sb("mx", [128, 4])
    p.consts()
    ftd = p.dram("ftd", [64, T])

    p.dma(SCt[:, :, :], sc.rearrange("(c p) n -> p c n", p=128))
    p.dma(GS64[:, :], gsf)
    p.dma(BV64[:, :], bvf)
    p.dma(bdn[:, :], bdown)

    p.red(CN[:, 0:1], GS64[:, :], OP.max)
    p.mm(p.PZa[0:1, 0:1], CN[:, 0:1], p.onescol[0:64, :])
    p.mset(LOHI[:, 0:1], 0.0)
    p.red(CN[:, 0:1], GS64[:, :], OP.max)
    p.mm(p.PZa[0:1, 0:1], CN[:, 0:1], p.onescol[0:64, :])
    p.ts(LOHI[:, 1:2], p.PZa[0:1, 0:1], 1.0, OP.add)
    for it in range(60):
        p.tt(LOHI[:, 2:3], LOHI[:, 0:1], LOHI[:, 1:2], OP.add)
        p.ts(LOHI[:, 2:3], LOHI[:, 2:3], 0.5, OP.mult)
        p.bc(CN[:, 1:2], LOHI[0:1, 2:3], w=1, npart=64)
        p.ts(KP[:, :], GS64[:, :], CN[:, 1:2], OP.is_gt)
        p.red(CN[:, 2:3], KP[:, :], OP.add)
        p.mm(p.PZa[0:1, 0:1], CN[:, 2:3], p.onescol[0:64, :])
        p.ts(LOHI[:, 3:4], p.PZa[0:1, 0:1], 4096.0, OP.is_ge)
        p.tt(LOHI[:, 4:5], LOHI[:, 2:3], LOHI[:, 0:1], OP.subtract)
        p.tt(LOHI[:, 4:5], LOHI[:, 4:5], LOHI[:, 3:4], OP.mult)
        p.tt(LOHI[:, 0:1], LOHI[:, 0:1], LOHI[:, 4:5], OP.add)
        p.tt(LOHI[:, 5:6], LOHI[:, 1:2], LOHI[:, 2:3], OP.subtract)
        p.tt(LOHI[:, 5:6], LOHI[:, 5:6], LOHI[:, 3:4], OP.mult)
        p.tt(LOHI[:, 1:2], LOHI[:, 2:3], LOHI[:, 5:6], OP.add)
    p.bc(CN[:, 1:2], LOHI[0:1, 0:1], w=1, npart=64)
    p.ts(KP[:, :], GS64[:, :], CN[:, 1:2], OP.is_gt)
    p.tt(KS1[:, :], BV64[:, :], KP[:, :], OP.mult)
    p.ts(KS2[:, :], KP[:, :], -1e30, OP.mult, s2=1e30, op1=OP.add)
    p.tt(KS1[:, :], KS1[:, :], KS2[:, :], OP.add)
    p.red(TH[:, :], KS1[:, :], OP.min)
    p.tr(THR[:, :], TH[:, :], n_out=1, n_in=64)
    p.bc(THB[:, :], THR[0:1, :], w=64)

    for t in range(NT):
        p.tt(F0[:, :], SCt[:, t, :], THB[:, :], OP.is_ge)
        p.tt(F0[:, :], SCt[:, t, :], F0[:, :], OP.mult)
        p.red(RS[:, :], F0[:, :], OP.add)
        p.ts(RS[:, :], RS[:, :], 1e-6, OP.add)
        p.recip(RS[:, :], RS[:, :])
        p.ts(F0[:, :], F0[:, :], RS[:, 0:1], OP.mult)
        p.tr(FST[:, t, :], F0[:, :], n_out=64, n_in=128)
    p.dma(ftd, FST[:, :, :].rearrange("p c n -> p (c n)"))

    for half in range(2):
        toff = half * HALF
        ttiles = range(half * 8, half * 8 + 8)
        p.dma(X[:, :, :], xn[toff:toff + HALF, :].rearrange("(c p) n -> p c n", p=128))
        for ti, t in enumerate(ttiles):
            for kc in range(8):
                p.tr(XNT[:, kc, ti * 128:(ti + 1) * 128], X[:, ti, kc * 128:(kc + 1) * 128])
        p.dma(X[:, :, :], resid[toff:toff + HALF, :].rearrange("(c p) n -> p c n", p=128))
        for od in range(8):
            p.mset(OT[:, od, :], 0.0)
        for od in range(8):
            for tc in range(2):
                p.mm(p.P0[:, 0:512], bdn[:, od * 128:(od + 1) * 128],
                     FST[:, half * 8 + tc * 4: half * 8 + tc * 4 + 4, :]
                     .rearrange("p c n -> p (c n)"),
                     start=True, stop=True)
                p.tt(OT[:, od, tc * 512:(tc + 1) * 512], OT[:, od, tc * 512:(tc + 1) * 512],
                     p.P0[:, 0:512], OP.add)
        for e in range(64):
            p.dma(Wu[:, :, :], wup[e].rearrange("(c p) n -> p c n", p=128))
            p.dma(Wg[:, :, :], wgate[e].rearrange("(c p) n -> p c n", p=128))
            p.dma(Wd[:, :, :], wdown[e].rearrange("(c p) n -> p c n", p=128))
            p.dma(BR2[0:1, 0:256], bup[e:e + 1, :])
            p.dma(BR2[0:1, 256:512], bgate[e:e + 1, :])
            for oc in range(2):
                p.tr(buc[:, oc:oc + 1], BR2[0:1, oc * 128:(oc + 1) * 128],
                     n_out=128, n_in=1)
                p.tr(bgc[:, oc:oc + 1], BR2[0:1, 256 + oc * 128:256 + (oc + 1) * 128],
                     n_out=128, n_in=1)
            p.dma(frow[:, :], ftd[e:e + 1, :])
            for oc in range(2):
                for tc in range(2):
                    tsl = slice(tc * 512, tc * 512 + 512)
                    gsl = slice(toff + tc * 512, toff + tc * 512 + 512)
                    for kc in range(8):
                        p.mm(p.P0[:, 0:512], Wu[:, kc, oc * 128:(oc + 1) * 128],
                             XNT[:, kc, tsl], start=(kc == 0), stop=(kc == 7))
                    p.ts(UPt[:, :], p.P0[:, 0:512], buc[:, oc:oc + 1], OP.add)
                    p.act(UPt[:, :], UPt[:, :], AF.Silu)
                    for kc in range(8):
                        p.mm(p.P1[:, 0:512], Wg[:, kc, oc * 128:(oc + 1) * 128],
                             XNT[:, kc, tsl], start=(kc == 0), stop=(kc == 7))
                    p.ts(GTt[:, :], p.P1[:, 0:512], bgc[:, oc:oc + 1], OP.add)
                    p.bc(FB[:, :], frow[0:1, gsl], w=512)
                    p.tt(H[:, oc, tsl], UPt[:, :], GTt[:, :], OP.mult)
                    p.tt(H[:, oc, tsl], H[:, oc, tsl], FB[:, :], OP.mult)
            for od in range(8):
                for tc in range(2):
                    tsl = slice(tc * 512, tc * 512 + 512)
                    for kc in range(2):
                        p.mm(p.P0[:, 0:512], Wd[:, kc, od * 128:(od + 1) * 128],
                             H[:, kc, tsl], start=(kc == 0), stop=(kc == 1))
                    p.tt(OT[:, od, tsl], OT[:, od, tsl], p.P0[:, 0:512], OP.add)
        for ti in range(8):
            for od in range(8):
                p.tr(s1[:, 0:128], OT[:, od, ti * 128:(ti + 1) * 128])
                p.tt(X[:, ti, od * 128:(od + 1) * 128], X[:, ti, od * 128:(od + 1) * 128],
                     s1[:, 0:128], OP.add)
        p.dma(yo[toff:toff + HALF, :].rearrange("(c p) n -> p c n", p=128), X[:, :, :])
    return p.finish()


def build_attn_b():
    p = PB()
    DIN = 4352
    NH = 32
    HD = 64
    DS = 64
    u = p.inp("u", [T, D])
    w_in = p.inp("w_in", [D, DIN])
    w_out = p.inp("w_out", [2048, D])
    dt_bias = p.inp("dt_bias", [1, NH])
    d_par = p.inp("d_par", [1, NH])
    bng = p.inp("bng", [1, DS]); bnb = p.inp("bnb", [1, DS])
    cng = p.inp("cng", [1, DS]); cnb = p.inp("cnb", [1, DS])
    mincl = p.inp("mincl", [128, 128])
    mdiag = p.inp("mdiag", [128, 128])
    u128_i = p.inp("u128", [128, 128])
    yo = p.outp("y", [T, D])

    XNT = p.sb("XNT", [128, 8, 2048])
    BCt = p.sb("BCt", [128, NT, 256])
    Bn = p.sb("Bn", [128, NT, 64]); Cn = p.sb("Cn", [128, NT, 64])
    BrT = p.sb("BrT", [64, NT, 128]); CrT = p.sb("CrT", [64, NT, 128])
    DT = p.sb("DT", [128, NT, 32]); TRP = p.sb("TRP", [128, NT, 32])
    CL = p.sb("CL", [128, NT, 32]); ECL = p.sb("ECL", [128, NT, 32])
    R1 = p.sb("R1", [128, NT, 32]); S2 = p.sb("S2", [128, NT, 32])
    PH = p.sb("PH", [128, NT, 32])
    CAR = p.sb("CAR", [1, 32])
    HS = p.sb("HS", [64, NH, HD])
    Gsb = p.sb("Gsb", [128, 128])
    Wt = p.sb("Wt", [128, 128]); WtT = p.sb("WtT", [128, 128])
    B1 = p.sb("B1", [128, 128])
    EB = p.sb("EB", [64, 128]); CsT = p.sb("CsT", [64, 128])
    Mi = p.sb("Mi", [128, 128]); Md = p.sb("Md", [128, 128])
    WS = p.sb("WS", [128, 8, 512])
    SCR2 = p.sb("SCR2", [128, 2048])
    XT = p.sb("XT", [128, 1024])
    TT32 = p.sb("TT32", [32, 128])
    r1row = p.sb("r1row", [1, 2048]); s2row = p.sb("s2row", [1, 2176])
    erow = p.sb("erow", [1, 2048]); clrow = p.sb("clrow", [1, 2048])
    lrow = p.sb("lrow", [1, 128])
    MADD = p.sb("MADD", [128, 128])
    madd_i = p.inp("madd", [128, 128])
    drow = p.sb("drow", [1, NH]); DB2 = p.sb("DB2", [128, NH])
    dtb = p.sb("dtb", [1, NH]); DTB = p.sb("DTB", [128, NH])
    g64 = p.sb("g64", [1, 4 * 64]); GB = p.sb("GB", [128, 4 * 64])
    s1 = p.sb("s1", [128, 1024]); s2_ = p.sb("s2_", [128, 1024])
    SCR = BCt
    OUTT = s2_
    sm1 = p.sb("sm1", [128, 32]); sm2 = p.sb("sm2", [128, 32]); sm3 = p.sb("sm3", [128, 32])
    smi = p.sb("smi", [128, 32], I32)
    ccol = p.sb("ccol", [128, 1]); crow = p.sb("crow", [1, 128])
    dpc = p.sb("dpc", [64, 1])
    U128 = p.sb("U128", [128, 128])
    p.sb_mx = p.sb("mx", [128, 4])
    p.consts()
    r1d = p.dram("r1d", [32, T])
    s2d = p.dram("s2d", [32, T + 128])
    ed = p.dram("ed", [32, T])
    cld = p.dram("cld", [32, T])

    p.dma(U128[:, :], u128_i)
    p.dma(Mi[:, :], mincl)
    p.dma(Md[:, :], mdiag)
    p.dma(MADD[:, :], madd_i)
    p.dma(dtb[:, :], dt_bias)
    p.dma(drow[:, :], d_par)
    p.dma(g64[0:1, 0:64], bng); p.dma(g64[0:1, 64:128], bnb)
    p.dma(g64[0:1, 128:192], cng); p.dma(g64[0:1, 192:256], cnb)

    mx = p.sb_mx
    for t in range(NT):
        p.dma(XT[:, :], u[t * 128:(t + 1) * 128, :])
        p.red(mx[:, 0:1], XT[:, :], OP.add)
        p.ts(mx[:, 0:1], mx[:, 0:1], 1.0 / D, OP.mult)
        p.ts(s1[:, :], XT[:, :], mx[:, 0:1], OP.subtract)
        p.act(s2_[:, :], s1[:, :], AF.Square)
        p.red(mx[:, 1:2], s2_[:, :], OP.add)
        p.ts(mx[:, 1:2], mx[:, 1:2], 1.0 / D, OP.mult, s2=EPS_LN, op1=OP.add)
        p.act(mx[:, 2:3], mx[:, 1:2], AF.Sqrt)
        p.recip(mx[:, 2:3], mx[:, 2:3])
        p.ts(s1[:, :], s1[:, :], mx[:, 2:3], OP.mult)
        for kc in range(8):
            p.tr(XNT[:, kc, t * 128:(t + 1) * 128], s1[:, kc * 128:(kc + 1) * 128])

    p.bc(DTB[:, :], dtb[0:1, :], w=NH)
    p.bc(DB2[:, :], drow[0:1, :], w=NH)
    p.bc(GB[:, :], g64[0:1, :], w=256)

    p.dma(WS[:, 0:4, :].rearrange("p c n -> p (c n)").rearrange("p (c n) -> p c n", n=256),
          w_in[:, 4096:4352].rearrange("(c p) n -> p c n", p=128))
    for t in range(NT):
        for kc in range(8):
            p.mm(p.P0[:, 0:256], XNT[:, kc, t * 128:(t + 1) * 128],
                 WS[:, kc // 2, (kc % 2) * 256:(kc % 2) * 256 + 256],
                 start=(kc == 0), stop=(kc == 7))
        p.cp(BCt[:, t, :], p.P0[:, 0:256])

    def lnproj(dst, off, goff, boff):
        for t in range(NT):
            xt = BCt[:, t, off:off + 64]
            p.red(p.sb_mx[:, 0:1], xt, OP.add)
            p.ts(p.sb_mx[:, 0:1], p.sb_mx[:, 0:1], 1.0 / 64, OP.mult)
            p.ts(dst[:, t, :], xt, p.sb_mx[:, 0:1], OP.subtract)
            p.act(s1[:, 0:64], dst[:, t, :], AF.Square)
            p.red(p.sb_mx[:, 1:2], s1[:, 0:64], OP.add)
            p.ts(p.sb_mx[:, 1:2], p.sb_mx[:, 1:2], 1.0 / 64, OP.mult, s2=1e-5, op1=OP.add)
            p.act(p.sb_mx[:, 2:3], p.sb_mx[:, 1:2], AF.Sqrt)
            p.recip(p.sb_mx[:, 2:3], p.sb_mx[:, 2:3])
            p.ts(dst[:, t, :], dst[:, t, :], p.sb_mx[:, 2:3], OP.mult)
            p.tt(dst[:, t, :], dst[:, t, :], GB[:, goff:goff + 64], OP.mult)
            p.tt(dst[:, t, :], dst[:, t, :], GB[:, boff:boff + 64], OP.add)

    lnproj(Bn, 0, 0, 64)
    lnproj(Cn, 64, 128, 192)

    def softplus(dst, src):
        p.act(sm1[:, :], src, AF.Abs)
        p.act(sm1[:, :], sm1[:, :], AF.Exp, scale=-1.0)
        p.act(sm1[:, :], sm1[:, :], AF.Ln, bias=1.0)
        p.act(sm2[:, :], src, AF.Relu)
        p.tt(dst, sm1[:, :], sm2[:, :], OP.add)

    TWO_PI = float(2 * np.pi)
    for t in range(NT):
        p.tt(sm3[:, :], BCt[:, t, 128:160], DTB[:, :], OP.add)
        softplus(DT[:, t, :], sm3[:, :])
        softplus(sm3[:, :], BCt[:, t, 160:192])
        p.ts(sm3[:, :], sm3[:, :], 1e-4, OP.max)
        p.tt(sm3[:, :], sm3[:, :], DT[:, t, :], OP.mult)
        p.ts(CL[:, t, :], sm3[:, :], -1.0, OP.mult)
        p.act(TRP[:, t, :], BCt[:, t, 192:224], AF.Sigmoid)

    for t in range(NT):
        p.red(p.sb_mx[:, 0:1], DT[:, t, :], OP.add)
        p.ts(p.sb_mx[:, 0:1], p.sb_mx[:, 0:1], 1.0 / 32, OP.mult)
        p.ts(sm3[:, :], BCt[:, t, 224:256], p.sb_mx[:, 0:1], OP.mult)
        if t == 0:
            p.mset(CAR[:, :], 0.0)
        p.mm(p.P0[:, 0:32], U128[:, :], sm3[:, :], start=True, stop=False)
        p.mm(p.P0[:, 0:32], p.ones[:1, :], CAR[0:1, :], start=False, stop=True)
        p.cp(PH[:, t, :], p.P0[:, 0:32])
        p.mm(p.P1[0:1, 0:32], p.onescol[:, :], sm3[:, :])
        p.tt(CAR[0:1, :], CAR[0:1, :], p.P1[0:1, 0:32], OP.add)

    for t in range(NT):
        ph = PH[:, t, :]
        p.ts(sm1[:, :], ph, 1.0 / TWO_PI, OP.mult)
        p.cp(smi[:, :], sm1[:, :])
        p.cp(sm2[:, :], smi[:, :])
        p.tt(sm1[:, :], sm1[:, :], sm2[:, :], OP.subtract)
        p.ts(sm2[:, :], sm1[:, :], 0.5, OP.is_gt)
        p.tt(sm1[:, :], sm1[:, :], sm2[:, :], OP.subtract)
        p.ts(sm2[:, :], sm1[:, :], -0.5, OP.is_lt)
        p.tt(sm1[:, :], sm1[:, :], sm2[:, :], OP.add)
        p.ts(sm1[:, :], sm1[:, :], TWO_PI, OP.mult)
        p.act(sm2[:, :], sm1[:, :], AF.Sin)
        p.act(sm3[:, :], sm1[:, :], AF.Sin, scale=0.5)
        p.act(sm3[:, :], sm3[:, :], AF.Square)
        p.ts(sm3[:, :], sm3[:, :], -2.0, OP.mult, s2=1.0, op1=OP.add)
        for M_ in (Bn, Cn):
            v = M_[:, t, :].rearrange("p (n two) -> p n two", two=2)
            ve = v[:, :, 0:1].rearrange("p n o -> p (n o)")
            vo = v[:, :, 1:2].rearrange("p n o -> p (n o)")
            p.tt(s1[:, 0:32], ve, sm3[:, :], OP.mult)
            p.tt(s1[:, 32:64], vo, sm2[:, :], OP.mult)
            p.tt(s1[:, 64:96], ve, sm2[:, :], OP.mult)
            p.tt(s1[:, 96:128], vo, sm3[:, :], OP.mult)
            p.tt(ve, s1[:, 0:32], s1[:, 32:64], OP.subtract)
            p.tt(vo, s1[:, 64:96], s1[:, 96:128], OP.add)

    for t in range(NT):
        p.act(sm1[:, :], CL[:, t, :], AF.Exp, scale=-1.0)
        p.tt(R1[:, t, :], TRP[:, t, :], DT[:, t, :], OP.mult)
        p.ts(sm3[:, :], TRP[:, t, :], -1.0, OP.mult, s2=1.0, op1=OP.add)
        p.tt(sm3[:, :], sm3[:, :], DT[:, t, :], OP.mult)
        p.tt(S2[:, t, :], sm1[:, :], sm3[:, :], OP.mult)
        p.mm(p.P0[:, 0:32], U128[:, :], CL[:, t, :], start=True, stop=True)
        p.cp(CL[:, t, :], p.P0[:, 0:32])
        p.act(ECL[:, t, :], CL[:, t, :], AF.Exp)
        p.tr(TT32[:, :], R1[:, t, :], n_out=32)
        p.dma(r1d[:, t * 128:(t + 1) * 128], TT32[:, :])
        p.tr(TT32[:, :], S2[:, t, :], n_out=32)
        p.dma(s2d[:, t * 128:(t + 1) * 128], TT32[:, :])
        p.tr(TT32[:, :], ECL[:, t, :], n_out=32)
        p.dma(ed[:, t * 128:(t + 1) * 128], TT32[:, :])
        p.tr(TT32[:, :], CL[:, t, :], n_out=32)
        p.dma(cld[:, t * 128:(t + 1) * 128], TT32[:, :])
        p.tr(BrT[:, t, :], Bn[:, t, :], n_out=64)
        p.tr(CrT[:, t, :], Cn[:, t, :], n_out=64)
    p.mset(TT32[:, :], 0.0)
    p.dma(s2d[:, T:T + 128], TT32[:, :])
    p.mset(HS[:, :, :], 0.0)

    YC = SCR[:, 0:8, :].rearrange("p c n -> p (c n)")
    ZC = SCR[:, 8:16, :].rearrange("p c n -> p (c n)")
    for c in range(NT):
        csl = slice(c * 128, (c + 1) * 128)
        for xc in range(4):
            p.dma(WS[:, :, :], w_in[:, 2048 + xc * 512:2048 + xc * 512 + 512]
                  .rearrange("(c p) n -> p c n", p=128))
            for kc in range(8):
                p.mm(p.P0[:, 0:512], XNT[:, kc, csl], WS[:, kc, :],
                     start=(kc == 0), stop=(kc == 7))
            p.cp(SCR2[:, xc * 512:(xc + 1) * 512], p.P0[:, 0:512])
        p.mm(p.P0[:, 0:128], CrT[:, c, :], BrT[:, c, :])
        p.cp(Gsb[:, :], p.P0[:, 0:128])
        for h in range(NH):
            hsl = slice(h * 64, h * 64 + 64)
            p.dma(r1row[:, :], r1d[h:h + 1, :])
            p.dma(s2row[:, 0:T + 128], s2d[h:h + 1, :])
            p.dma(erow[:, :], ed[h:h + 1, :])
            p.dma(clrow[:, :], cld[h:h + 1, :])
            LAM = s2_[:, 0:128]
            p.bc(LAM, clrow[0:1, csl], w=128)
            p.ts(LAM, LAM, -1.0, OP.mult)
            p.ts(LAM, LAM, CL[:, c, h:h + 1], OP.add)
            p.tt(LAM, LAM, MADD[:, :], OP.add)
            p.act(LAM, LAM, AF.Exp)
            p.tt(crow[:, :], r1row[0:1, csl], s2row[0:1, c * 128 + 1:c * 128 + 129], OP.add)
            p.bc(B1[:, :], crow[0:1, :], w=128)
            p.ts(Wt[:, :], Md[:, :], R1[:, c, h:h + 1], OP.mult)
            p.tt(s1[:, 0:128], Md[:, :], B1[:, :], OP.mult)
            p.tt(s1[:, 0:128], B1[:, :], s1[:, 0:128], OP.subtract)
            p.tt(Wt[:, :], Wt[:, :], s1[:, 0:128], OP.add)
            p.tt(Wt[:, :], Wt[:, :], Gsb[:, :], OP.mult)
            p.tt(Wt[:, :], Wt[:, :], LAM, OP.mult)
            p.tr(WtT[:, :], Wt[:, :])
            p.bc(EB[:, :], erow[0:1, csl], w=128, npart=64)
            p.tt(CsT[:, :], CrT[:, c, :], EB[:, :], OP.mult)
            p.mm(p.P0[:, 0:64], WtT[:, :], SCR2[:, hsl], start=True, stop=False)
            p.mm(p.P0[:, 0:64], CsT[:, :], HS[:, h, :], start=False, stop=True)
            ysl = SCR[:, h // 4, (h % 4) * 64:(h % 4) * 64 + 64]
            p.cp(ysl, p.P0[:, 0:64])
            p.ts(s1[:, 0:64], SCR2[:, hsl], DB2[:, h:h + 1], OP.mult)
            p.tt(ysl, ysl, s1[:, 0:64], OP.add)
            p.ts(lrow[:, :], clrow[0:1, csl], -1.0, OP.mult)
            p.ts(lrow[:, :], lrow[:, :], clrow[0:1, c * 128 + 127:c * 128 + 128], OP.add)
            p.act(lrow[:, :], lrow[:, :], AF.Exp)
            p.tt(crow[:, :], crow[:, :], lrow[:, :], OP.mult)
            p.tr(ccol[:, :], crow[:, :], n_out=128, n_in=1)
            p.ts(s1[:, 0:64], Bn[:, c, :], ccol[:, 0:1], OP.mult)
            p.mm(p.P1[0:64, 0:64], s1[:, 0:64], SCR2[:, hsl])
            p.bc(dpc[:, :], erow[0:1, c * 128 + 127:c * 128 + 128], w=1, npart=64)
            p.ts(HS[:, h, :], HS[:, h, :], dpc[:, 0:1], OP.mult)
            p.tt(HS[:, h, :], HS[:, h, :], p.P1[0:64, 0:64], OP.add)
        for xc in range(4):
            p.dma(WS[:, :, :], w_in[:, xc * 512:xc * 512 + 512]
                  .rearrange("(c p) n -> p c n", p=128))
            for kc in range(8):
                p.mm(p.P0[:, 0:512], XNT[:, kc, csl], WS[:, kc, :],
                     start=(kc == 0), stop=(kc == 7))
            p.act(ZC[:, xc * 512:(xc + 1) * 512], p.P0[:, 0:512], AF.Silu)
        p.tt(YC, YC, ZC, OP.mult)
        for kt in range(16):
            p.tr(WS[:, 4 + kt // 4, (kt % 4) * 128:(kt % 4) * 128 + 128],
                 YC[:, kt * 128:(kt + 1) * 128])
        p.dma(OUTT[:, :], u[c * 128:(c + 1) * 128, :])
        for od2 in range(2):
            for kt in range(16):
                p.dma(WS[:, 0, :], w_out[kt * 128:(kt + 1) * 128, od2 * 512:(od2 + 1) * 512])
                p.mm(p.P1[:, 0:512],
                     WS[:, 4 + kt // 4, (kt % 4) * 128:(kt % 4) * 128 + 128],
                     WS[:, 0, :], start=(kt == 0), stop=(kt == 15))
            p.tt(OUTT[:, od2 * 512:(od2 + 1) * 512], OUTT[:, od2 * 512:(od2 + 1) * 512],
                 p.P1[:, 0:512], OP.add)
        p.dma(yo[c * 128:(c + 1) * 128, :], OUTT[:, :])
    return p.finish()


_cache = {}


def _get(name, fn):
    if name not in _cache:
        _cache[name] = fn()
    return _cache[name]


def _run(nc, maps):
    res = run_bass_kernel_spmd(nc, maps, core_ids=list(range(NCORE)))
    return res.results


def _ffn(x_full, CONST, W):
    nc1 = _get("f1", build_f1)
    maps = []
    for c in range(NCORE):
        sl = x_full[c * T:(c + 1) * T]
        maps.append(dict(xb=sl, wr=W["Wr"], br=W["br"][None, :], **CONST))
    r1 = _run(nc1, maps)
    xn = np.concatenate([r["xn"] for r in r1], 0)
    sc = np.concatenate([r["sc"] for r in r1], 0)
    nc2 = _get("f2", build_f2)
    scT = np.ascontiguousarray(sc.T)
    maps = [dict(sc8=np.ascontiguousarray(scT[c * 8:(c + 1) * 8]), **CONST)
            for c in range(NCORE)]
    r2 = _run(nc2, maps)
    gs = np.concatenate([r["gs8"] for r in r2], 0)
    bv = np.concatenate([r["bv8"] for r in r2], 0)
    nc3 = _get("f3", build_f3)
    maps = []
    for c in range(NCORE):
        maps.append(dict(xn=xn[c * T:(c + 1) * T], sc=sc[c * T:(c + 1) * T],
                         gsf=gs, bvf=bv, resid=x_full[c * T:(c + 1) * T],
                         wup=W["Wup"], wgate=W["Wgate"], wdown=W["Wdown"],
                         bup=W["bup"], bgate=W["bgate"], bdown=W["bdown"], **CONST))
    r3 = _run(nc3, maps)
    return np.concatenate([r["y"] for r in r3], 0)


def kernel(**inp):
    x = np.ascontiguousarray(inp["L0_attn_a_x"].astype(np.float32))
    eye = np.eye(128, dtype=np.float32)
    ones = np.ones((1, 128), np.float32)
    onescol = np.ones((128, 1), np.float32)
    CONST = dict(c_eye_i=eye, c_ones_i=ones, c_onescol_i=onescol)

    amask = np.full((128, 896), -1e30, np.float32)
    ii = np.arange(128)[:, None]
    jj = np.arange(896)[None, :]
    amask[jj <= ii + 384] = 0.0
    nca = _get("a", build_attn_a)
    maps = []
    for c in range(NCORE):
        maps.append(dict(xb=x[c], wq=inp["L0_attn_a_w_q"], wk=inp["L0_attn_a_w_k"],
                         wv=inp["L0_attn_a_w_v"], bq=inp["L0_attn_a_b_q"][None, :],
                         bk=inp["L0_attn_a_b_k"][None, :], bv=inp["L0_attn_a_b_v"][None, :],
                         beta=inp["L0_attn_a_beta"][None, :], amask=amask, **CONST))
    ra = _run(nca, maps)
    x1 = np.stack([r["y"] for r in ra], 0).reshape(NCORE * T, D)

    WF = dict(Wr=inp["L0_ffn_Wr"], br=inp["L0_ffn_br"], Wup=inp["L0_ffn_Wup"],
              Wgate=inp["L0_ffn_Wgate"], Wdown=inp["L0_ffn_Wdown"],
              bup=inp["L0_ffn_bup"], bgate=inp["L0_ffn_bgate"],
              bdown=inp["L0_ffn_bdown"])
    x2 = _ffn(x1, CONST, WF)

    u128 = np.triu(np.ones((128, 128), np.float32))
    mincl = np.tril(np.ones((128, 128), np.float32))
    mdiag = np.eye(128, dtype=np.float32)
    ncb = _get("b", build_attn_b)
    maps = []
    for c in range(NCORE):
        maps.append(dict(u=x2[c * T:(c + 1) * T], w_in=inp["L0_attn_b_W_in"],
                         w_out=inp["L0_attn_b_W_out"],
                         dt_bias=inp["L0_attn_b_dt_bias"][None, :],
                         d_par=inp["L0_attn_b_D_param"][None, :],
                         bng=inp["L0_attn_b_Bn_g"][None, :], bnb=inp["L0_attn_b_Bn_b"][None, :],
                         cng=inp["L0_attn_b_Cn_g"][None, :], cnb=inp["L0_attn_b_Cn_b"][None, :],
                         mincl=mincl, mdiag=mdiag, u128=u128,
                         madd=(mincl - 1.0) * 1e30, **CONST))
    rb = _run(ncb, maps)
    x3 = np.concatenate([r["y"] for r in rb], 0)

    x4 = _ffn(x3, CONST, WF)
    return x4.reshape(NCORE, T, D)
